# version 01
# baseline (speedup 1.0000x reference)
import numpy as np
import ml_dtypes
from contextlib import ExitStack

import concourse.bass as bass
import concourse.mybir as mybir
from concourse.bass_utils import run_bass_kernel_spmd

F32 = mybir.dt.float32
BF16 = mybir.dt.bfloat16
AF = mybir.ActivationFunctionType
ALU = mybir.AluOpType
AX = mybir.AxisListType

D = 1024
NH = 16
TOWN = 2048
TALL = 8192
E = 15
F = 1024
FS = 2048
NEG = -1e30

ENGS = ("sync", "gpsimd", "vector", "scalar", "tensor")


class Sem:
    def __init__(self, h):
        self.h = h
        self.n = 0


class Prog:
    def __init__(self, nc, stack):
        self.nc = nc
        self.stack = stack
        self.q = {e: [] for e in ENGS}

    def sem(self, name):
        return Sem(self.stack.enter_context(self.nc.semaphore(name)))

    def wait(self, eng, sem, v=None):
        tgt = sem.n if v is None else v
        if tgt <= 0:
            return
        self.q[eng].append(lambda e, s=sem.h, t=tgt: e.wait_ge(s, t))

    def op(self, eng, build, inc=None, n=1):
        if inc is not None:
            inc.n += n
            self.q[eng].append(lambda e, b=build, s=inc.h, k=n: b(e).then_inc(s, k))
        else:
            self.q[eng].append(lambda e, b=build: b(e))

    def vdrain(self):
        self.q["vector"].append(lambda e: e.drain())

    def vsync(self, sem):
        self.q["vector"].append(lambda e: e.drain())
        self.op("vector", lambda v: v.nop(), sem)

    def barrier(self, sems):
        self.q["vector"].append(lambda e: e.drain())
        vals = {}
        for eng in ENGS:
            s = sems[eng]
            s.n += 1
            self.q[eng].append(lambda e, h=s.h: e.nop().then_inc(h, 1))
            vals[eng] = s.n
        for eng in ENGS:
            for o in ENGS:
                if o != eng:
                    self.q[eng].append(lambda e, h=sems[o].h, t=vals[o]: e.wait_ge(h, t))

    def emit(self, block):
        for name in ENGS:
            fns = self.q[name]
            def run(e, fns=fns):
                for f in fns:
                    f(e)
            getattr(block, name)(run)


def _ln_block(P, sb_tensors, sems, xb, out_bf16, own_extra=None):
    pass



def build_l0(debug=False):
    nc = bass.Bass("TRN2", target_bir_lowering=False)
    xr = nc.dram_tensor("xr", [TALL, D], F32, kind="ExternalInput").ap()
    wqkv = nc.dram_tensor("wqkv", [D, 3 * D], F32, kind="ExternalInput").ap()
    bqkv_cols = nc.dram_tensor("bqkv_cols", [128, 16], F32, kind="ExternalInput").ap()
    bqkv_vrow = nc.dram_tensor("bqkv_vrow", [1, D], F32, kind="ExternalInput").ap()
    wproj = nc.dram_tensor("wproj", [D, D], F32, kind="ExternalInput").ap()
    bproj_row = nc.dram_tensor("bproj_row", [1, D], F32, kind="ExternalInput").ap()
    gamma_cols = nc.dram_tensor("gamma_cols", [128, 8], F32, kind="ExternalInput").ap()
    beta_cols = nc.dram_tensor("beta_cols", [128, 8], F32, kind="ExternalInput").ap()
    gatewT = nc.dram_tensor("gatewT", [D, E], F32, kind="ExternalInput").ap()
    fc1wT = nc.dram_tensor("fc1wT", [E, D, 2 * F], F32, kind="ExternalInput").ap()
    fc1b_xcols = nc.dram_tensor("fc1b_xcols", [128, E * 8], F32, kind="ExternalInput").ap()
    fc1b_gcols = nc.dram_tensor("fc1b_gcols", [128, E * 8], F32, kind="ExternalInput").ap()
    mult_cols = nc.dram_tensor("mult_cols", [128, E * 8], F32, kind="ExternalInput").ap()
    fc2wT = nc.dram_tensor("fc2wT", [E, F, D], F32, kind="ExternalInput").ap()
    fc2b = nc.dram_tensor("fc2b", [E, D], F32, kind="ExternalInput").ap()
    sfc1wT = nc.dram_tensor("sfc1wT", [D, 2 * FS], F32, kind="ExternalInput").ap()
    sfc1b_xcols = nc.dram_tensor("sfc1b_xcols", [128, 16], F32, kind="ExternalInput").ap()
    sfc1b_gcols = nc.dram_tensor("sfc1b_gcols", [128, 16], F32, kind="ExternalInput").ap()
    smult_cols = nc.dram_tensor("smult_cols", [128, 16], F32, kind="ExternalInput").ap()
    sfc2wT = nc.dram_tensor("sfc2wT", [FS, D], F32, kind="ExternalInput").ap()
    sfc2b_row = nc.dram_tensor("sfc2b_row", [1, D], F32, kind="ExternalInput").ap()
    iden = nc.dram_tensor("iden", [128, 128], F32, kind="ExternalInput").ap()
    sel = nc.dram_tensor("sel", [E, E * 128], F32, kind="ExternalInput").ap()
    x1own = nc.dram_tensor("x1own", [TOWN, D], F32, kind="ExternalOutput").ap()
    dbg = {}
    if debug:
        for nm, shp, dt in (("d_xn2f", [128, 4096], BF16), ("d_q", [128, 2048], BF16),
                            ("d_negm", [128, 256], F32), ("d_ctxf", [128, 16384], BF16),
                            ("d_attn", [TOWN, D], F32), ("d_comb", [E, TOWN], F32),
                            ("d_ymoe", [TOWN, D], F32), ("d_xnf", [128, 16384], BF16)):
            dbg[nm] = nc.dram_tensor(nm, shp, dt, kind="ExternalOutput").ap()
    k0sp = nc.dram_tensor("k0sp", [NH, 64, TALL], BF16, kind="Internal").ap()
    v0sp = nc.dram_tensor("v0sp", [64, 128, 1040], BF16, kind="Internal").ap()
    attn_sp = nc.dram_tensor("attn_sp", [TOWN, D], F32, kind="Internal").ap()
    y_sp = nc.dram_tensor("y_sp", [TOWN, D], F32, kind="Internal").ap()

    with ExitStack() as stack:
        P = Prog(nc, stack)
        sb = lambda *a, **k: stack.enter_context(nc.sbuf_tensor(*a, **k))
        sld = P.sem("sld"); sw = P.sem("sw"); ssp = P.sem("ssp")
        sT = P.sem("sT"); sV = P.sem("sV"); sS = P.sem("sS")
        sO = P.sem("sO"); smr = P.sem("smr")
        bsem = {e: P.sem("b_" + e) for e in ENGS}

        identb = sb("identb", [128, 128], BF16)
        identf = sb("identf", [128, 128], F32)
        onesr_b = sb("onesr_b", [1, 512], BF16)
        onesr_f = sb("onesr_f", [1, 512], F32)
        eps_sb = sb("eps_sb", [128, 1], F32)
        xnf_sb = sb("xnf_sb", [128, 8 * TOWN], BF16)
        qfm_sb = sb("qfm_sb", [128, 8 * TOWN], BF16)
        negm_sb = sb("negm_sb", [128, 256], F32)
        st_sb = sb("st_sb", [128, 64], F32)
        C = lambda j, i: st_sb[:, j + i:j + i + 1]

        ph0 = ExitStack()
        sb0 = lambda *a, **k: ph0.enter_context(nc.sbuf_tensor(*a, **k))
        wqkv_sb = sb0("wqkv_sb", [128, 8 * 3072], BF16)
        bqkvc_sb = sb0("bqkvc_sb", [128, 16], F32)
        bqkvv_sb = sb0("bqkvv_sb", [1, D], BF16)
        gamma_sb = sb0("gamma_sb", [128, 8], F32)
        beta_sb = sb0("beta_sb", [128, 8], F32)
        x_sb = [sb0(f"x_sb{i}", [128, 4096], F32) for i in range(2)]
        xn_sb = sb0("xn_sb", [128, 4096], F32)
        v2_sb = sb0("v2_sb", [128, 4096], BF16)
        sq_sb = sb0("sq_sb", [128, 1024], F32)
        xn2f_sb = sb0("xn2f_sb", [128, 8 * 512], BF16)
        ksb = [sb0(f"ksb{i}", [128, 512], BF16) for i in range(2)]
        vtile = [sb0(f"vtile{i}", [128, 1040], BF16) for i in range(2)]

        with nc.Block() as block:
            P.op("gpsimd", lambda g: g.dma_start(out=wqkv_sb[:].rearrange("p (k c) -> p k c", k=8),
                                                 in_=wqkv.rearrange("(k p) c -> p k c", p=128)), sw, 16)
            P.op("sync", lambda s: s.dma_start(out=bqkvc_sb[:], in_=bqkv_cols), sld, 16)
            P.op("gpsimd", lambda g: g.dma_start(out=bqkvv_sb[:], in_=bqkv_vrow), sw, 16)
            P.op("sync", lambda s: s.dma_start(out=gamma_sb[:], in_=gamma_cols), sld, 16)
            P.op("sync", lambda s: s.dma_start(out=beta_sb[:], in_=beta_cols), sld, 16)
            P.op("gpsimd", lambda g: g.dma_start(out=identb[:], in_=iden), sw, 16)
            P.op("sync", lambda s: s.dma_start(out=identf[:], in_=iden), sld, 16)
            P.op("vector", lambda v: v.memset(onesr_b[:], 1.0), sV)
            P.op("vector", lambda v: v.memset(onesr_f[:], 1.0), sV)
            P.op("vector", lambda v: v.memset(eps_sb[:], 1e-5), sV)
            for i in range(2):
                P.op("vector", lambda v, i=i: v.memset(
                    bass.AP(vtile[i], 64, [[1040, 128], [65, 16], [1, 1]]), 1.0), sV)
            w_ready = sw.n
            ld_ready = sld.n

            with (nc.psum_tensor("pt0", [128, 1024], BF16) as pt0,
                  nc.psum_tensor("pt1", [128, 1024], BF16) as pt1,
                  nc.psum_tensor("pq0", [128, 512], F32) as pq0,
                  nc.psum_tensor("pq1", [128, 512], F32) as pq1,
                  nc.psum_tensor("pv0", [128, 512], F32) as pv0,
                  nc.psum_tensor("pv1", [128, 512], F32) as pv1):
                pts = [pt0, pt1]; pqs = [pq0, pq1]; pvs = [pv0, pv1]
                x_free = [0, 0]; ks_free = [0, 0]; vt_free = [0, 0]
                pt_free = [0, 0]; pq_free = [0, 0]; pv_free = [0, 0]
                x_loaded = [0, 0]

                def ld_x(tb, buf):
                    P.wait("sync", sV, x_free[buf])
                    P.op("sync", lambda s, tb=tb, buf=buf: s.dma_start(
                        out=x_sb[buf][:].rearrange("p (i d) -> p i d", i=4),
                        in_=xr[tb * 512:(tb + 1) * 512, :].rearrange("(i p) d -> p i d", p=128)),
                        sld, 16)
                    x_loaded[buf] = sld.n

                def ln_stats(xb, base):
                    for i in range(4):
                        P.op("scalar", lambda s, i=i, xb=xb: s.activation(
                            sq_sb[:], xb[:, i * 1024:(i + 1) * 1024], AF.Square,
                            accum_out=C(base + 4, i)), sS)
                        P.op("vector", lambda v, i=i, xb=xb: v.reduce_sum(
                            C(base + 0, i), xb[:, i * 1024:(i + 1) * 1024], axis=AX.X), sV)
                    P.wait("vector", sS, sS.n)
                    P.vdrain()
                    for i in range(4):
                        P.op("vector", lambda v, i=i: v.tensor_scalar(
                            C(base + 8, i), C(base + 0, i), 1.0 / D, None, op0=ALU.mult), sV)
                        P.op("vector", lambda v, i=i: v.tensor_scalar(
                            C(48, i), C(base + 4, i), 1.0 / D, None, op0=ALU.mult), sV)
                    P.vdrain()
                    for i in range(4):
                        P.op("vector", lambda v, i=i: v.tensor_mul(
                            C(56, i), C(base + 8, i), C(base + 8, i)), sV)
                    P.vdrain()
                    for i in range(4):
                        P.op("vector", lambda v, i=i: v.tensor_sub(C(48, i), C(48, i), C(56, i)), sV)
                    P.vsync(sV)
                    P.wait("scalar", sV, sV.n)
                    P.op("scalar", lambda s: s.activation(
                        st_sb[:, 52:56], st_sb[:, 48:52], AF.Sqrt, bias=eps_sb[:, 0:1]), sS)
                    P.wait("vector", sS, sS.n)
                    P.op("vector", lambda v: v.reciprocal(st_sb[:, base + 12:base + 16],
                                                          st_sb[:, 52:56]), sV)
                    P.vdrain()

                ld_x(0, 0)
                for tb in range(16):
                    buf = tb % 2
                    if tb < 15:
                        ld_x(tb + 1, 1 - buf)
                    xb = x_sb[buf]
                    own = tb < 4
                    P.wait("scalar", sld, x_loaded[buf])
                    P.wait("vector", sld, x_loaded[buf])
                    ln_stats(xb, 0)
                    for i in range(4):
                        P.op("vector", lambda v, i=i, xb=xb: v.tensor_scalar(
                            xn_sb[:, i * 1024:(i + 1) * 1024], xb[:, i * 1024:(i + 1) * 1024],
                            C(8, i), C(12, i), op0=ALU.subtract, op1=ALU.mult), sV)
                    x_free[buf] = sV.n
                    P.wait("scalar", sV, sV.n)
                    ln_stats(xn_sb, 16)
                    for i in range(4):
                        P.op("vector", lambda v, i=i: v.tensor_scalar(
                            v2_sb[:, i * 1024:(i + 1) * 1024], xn_sb[:, i * 1024:(i + 1) * 1024],
                            C(24, i), C(28, i), op0=ALU.subtract, op1=ALU.mult), sV)
                    v2_ready = sV.n
                    P.wait("tensor", sV, v2_ready)
                    P.wait("tensor", sld, ld_ready)
                    for i in range(4):
                        for fc in range(8):
                            pi = (i * 8 + fc) % 2
                            P.wait("tensor", sV, pt_free[pi])
                            P.op("tensor", lambda t, i=i, fc=fc, pi=pi: t.transpose(
                                pts[pi][:, :128],
                                v2_sb[:, i * 1024 + fc * 128: i * 1024 + (fc + 1) * 128],
                                identb[:]), sT)
                            P.wait("vector", sT, sT.n)
                            P.op("vector", lambda v, i=i, fc=fc, pi=pi: v.tensor_scalar(
                                xn2f_sb[:, fc * 512 + i * 128: fc * 512 + (i + 1) * 128],
                                pts[pi][:, :128], gamma_sb[:, fc:fc + 1], beta_sb[:, fc:fc + 1],
                                op0=ALU.mult, op1=ALU.add), sV)
                            pt_free[pi] = sV.n
                    if own:
                        for i in range(4):
                            P.op("vector", lambda v, i=i: v.tensor_copy(
                                v2_sb[:, i * 1024:(i + 1) * 1024],
                                xn_sb[:, i * 1024:(i + 1) * 1024]), sV)
                        P.wait("tensor", sV, sV.n)
                        for i in range(4):
                            for fc in range(8):
                                pi = (i * 8 + fc) % 2
                                P.wait("tensor", sV, pt_free[pi])
                                P.op("tensor", lambda t, i=i, fc=fc, pi=pi: t.transpose(
                                    pts[pi][:, :128],
                                    v2_sb[:, i * 1024 + fc * 128: i * 1024 + (fc + 1) * 128],
                                    identb[:]), sT)
                                P.wait("vector", sT, sT.n)
                                tg = tb * 512 + i * 128
                                P.op("vector", lambda v, fc=fc, tg=tg, pi=pi: v.tensor_copy(
                                    xnf_sb[:, fc * TOWN + tg: fc * TOWN + tg + 128],
                                    pts[pi][:, :128]), sV)
                                pt_free[pi] = sV.n
                    P.vsync(sV)
                    xn2f_ready = sV.n
                    P.wait("tensor", sV, xn2f_ready)
                    P.wait("tensor", sw, w_ready)
                    ofcs = (list(range(0, 8)) if own else []) + list(range(8, 16))
                    for ofc in ofcs:
                        pi = ofc % 2
                        P.wait("tensor", sV, pq_free[pi])
                        if ofc >= 8:
                            P.wait("tensor", ssp, ks_free[pi])
                        for kc in range(8):
                            P.op("tensor", lambda t, ofc=ofc, kc=kc, pi=pi: t.matmul(
                                pqs[pi][:],
                                wqkv_sb[:, kc * 3072 + ofc * 128: kc * 3072 + (ofc + 1) * 128],
                                xn2f_sb[:, kc * 512:(kc + 1) * 512],
                                start=(kc == 0), stop=(kc == 7)), sT)
                        P.wait("vector", sT, sT.n)
                        if ofc < 8:
                            tg = tb * 512
                            P.op("vector", lambda v, ofc=ofc, tg=tg, pi=pi: v.tensor_scalar(
                                qfm_sb[:, ofc * TOWN + tg: ofc * TOWN + tg + 512],
                                pqs[pi][:], bqkvc_sb[:, ofc:ofc + 1], 0.125,
                                op0=ALU.add, op1=ALU.mult), sV)
                            pq_free[pi] = sV.n
                        else:
                            c = ofc - 8
                            P.wait("vector", ssp, ks_free[pi])
                            P.op("vector", lambda v, ofc=ofc, pi=pi: v.tensor_scalar(
                                ksb[pi][:], pqs[pi][:], bqkvc_sb[:, ofc:ofc + 1], None,
                                op0=ALU.add), sV)
                            pq_free[pi] = sV.n
                            P.wait("sync", sV, sV.n)
                            P.op("sync", lambda s, c=c, tb=tb, pi=pi: s.dma_start(
                                out=k0sp[2 * c, :, tb * 512:(tb + 1) * 512], in_=ksb[pi][0:64, :]),
                                ssp, 16)
                            P.op("sync", lambda s, c=c, tb=tb, pi=pi: s.dma_start(
                                out=k0sp[2 * c + 1, :, tb * 512:(tb + 1) * 512], in_=ksb[pi][64:128, :]),
                                ssp, 16)
                            ks_free[pi] = ssp.n
                    for i in range(4):
                        vi = i % 2
                        P.wait("vector", ssp, vt_free[vi])
                        for half in range(2):
                            P.wait("tensor", sV, pv_free[half])
                            P.op("tensor", lambda t, half=half: t.matmul(
                                pvs[half][:], onesr_b[0:1, 0:128],
                                bqkvv_sb[0:1, half * 512:(half + 1) * 512],
                                start=True, stop=False), sT)
                            for kc in range(8):
                                P.op("tensor", lambda t, i=i, kc=kc, half=half: t.matmul(
                                    pvs[half][:],
                                    xn2f_sb[:, kc * 512 + i * 128: kc * 512 + (i + 1) * 128],
                                    wqkv_sb[:, kc * 3072 + 2048 + half * 512:
                                            kc * 3072 + 2048 + (half + 1) * 512],
                                    start=False, stop=(kc == 7)), sT)
                            P.wait("vector", sT, sT.n)
                            P.op("vector", lambda v, half=half, vi=vi: v.tensor_copy(
                                bass.AP(vtile[vi], half * 8 * 65, [[1040, 128], [65, 8], [1, 64]]),
                                pvs[half][:].rearrange("p (h d) -> p h d", h=8)), sV)
                            pv_free[half] = sV.n
                        P.wait("sync", sV, sV.n)
                        P.op("sync", lambda s, tb=tb, i=i, vi=vi: s.dma_start(
                            out=v0sp[tb * 4 + i], in_=vtile[vi][:]), ssp, 16)
                        vt_free[vi] = ssp.n
                P.barrier(bsem)
                if debug:
                    P.op("sync", lambda s: s.dma_start(out=dbg["d_xn2f"], in_=xn2f_sb[:]), sO, 16)
                    P.op("sync", lambda s: s.dma_start(out=dbg["d_q"], in_=qfm_sb[:, 0:2048]), sO, 16)
                    P.op("sync", lambda s: s.dma_start(out=dbg["d_xnf"], in_=xnf_sb[:]), sO, 16)
                    P.barrier(bsem)

            ph0.close()
            ph1 = ExitStack()
            kbuf = [ph1.enter_context(nc.sbuf_tensor(f"kbuf{i}", [128, 2048], BF16))
                    for i in range(2)]
            mraw_sb = ph1.enter_context(nc.sbuf_tensor("mraw_sb", [128, 1024], F32))
            with (nc.psum_tensor("psA0", [128, 2048], F32) as psA0,
                  nc.psum_tensor("psA1", [128, 2048], F32) as psA1):
                kb_free = [0, 0]; kb_loaded = [0, 0]

                def ld_kpair(pair, kblk, buf):
                    P.wait("sync", sT, kb_free[buf])
                    P.op("sync", lambda s, pair=pair, kblk=kblk, buf=buf: s.dma_start(
                        out=kbuf[buf][0:64, :],
                        in_=k0sp[2 * pair, :, kblk * 2048:(kblk + 1) * 2048]), sld, 16)
                    P.op("sync", lambda s, pair=pair, kblk=kblk, buf=buf: s.dma_start(
                        out=kbuf[buf][64:128, :],
                        in_=k0sp[2 * pair + 1, :, kblk * 2048:(kblk + 1) * 2048]), sld, 16)
                    kb_loaded[buf] = sld.n

                ld_kpair(0, 0, 0)
                dve_done = 0
                for step in range(32):
                    pair, kblk = divmod(step, 4)
                    buf = step % 2
                    if step + 1 < 32:
                        npair, nkblk = divmod(step + 1, 4)
                        ld_kpair(npair, nkblk, 1 - buf)
                    P.wait("tensor", sld, kb_loaded[buf])
                    for qt in range(16):
                        P.wait("tensor", sV, dve_done)
                        for nn in range(4):
                            P.op("tensor", lambda t, pair=pair, qt=qt, nn=nn, buf=buf: t.matmul(
                                psA0[:, nn * 512:(nn + 1) * 512],
                                qfm_sb[0:64, pair * TOWN + qt * 128: pair * TOWN + (qt + 1) * 128],
                                kbuf[buf][0:64, nn * 512:(nn + 1) * 512],
                                start=True, stop=True, tile_position=(0, 0)), sT)
                            P.op("tensor", lambda t, pair=pair, qt=qt, nn=nn, buf=buf: t.matmul(
                                psA1[:, nn * 512:(nn + 1) * 512],
                                qfm_sb[64:128, pair * TOWN + qt * 128: pair * TOWN + (qt + 1) * 128],
                                kbuf[buf][64:128, nn * 512:(nn + 1) * 512],
                                start=True, stop=True, tile_position=(64, 0)), sT)
                        mm_done = sT.n
                        P.wait("vector", sT, mm_done)
                        for hh, pp in ((0, psA0), (1, psA1)):
                            h = pair * 2 + hh
                            rc = (qt * 16 + h) * 4 + kblk
                            P.op("vector", lambda v, pp=pp, rc=rc: v.reduce_max(
                                mraw_sb[:, rc:rc + 1], pp[:], axis=AX.X), sV)
                        dve_done = sV.n
                    kb_free[buf] = sT.n
                P.vdrain()
                for col in range(256):
                    P.op("vector", lambda v, col=col: v.reduce_max(
                        negm_sb[:, col:col + 1], mraw_sb[:, col * 4:(col + 1) * 4],
                        axis=AX.X, negate=True), sV)
                P.vsync(sV)
                P.barrier(bsem)
                if debug:
                    P.op("sync", lambda s: s.dma_start(out=dbg["d_negm"], in_=negm_sb[:]), sO, 16)
                    P.barrier(bsem)

            ph1.close()
            ph2 = ExitStack()
            sb2 = lambda *a, **k: ph2.enter_context(nc.sbuf_tensor(*a, **k))
            ctxf_sb = sb2("ctxf_sb", [128, 8 * TOWN], BF16)
            khb = [[sb2(f"khb{j}_{i}", [65, 1024], BF16) for i in range(2)] for j in range(4)]
            qaug = [sb2(f"qaug{j}", [65, 512], BF16) for j in range(4)]
            vblk = [sb2(f"vblk{i}", [128, 8 * 1040], BF16) for i in range(2)]
            pexp = [sb2(f"pexp{i}", [128, 512], BF16) for i in range(2)]
            ctmp = sb2("ctmp", [128, 512], F32)
            l_sb = sb2("l_sb", [1, 512], F32)
            rl_sb = sb2("rl_sb", [1, 512], F32)
            for j in range(4):
                for i in range(2):
                    P.op("vector", lambda v, j=j, i=i: v.memset(khb[j][i][64:65, :], 1.0), sV)
            with (nc.psum_tensor("ctx0", [65, 512], F32) as ctx0,
                  nc.psum_tensor("ctx1", [65, 512], F32) as ctx1,
                  nc.psum_tensor("ctx2", [65, 512], F32) as ctx2,
                  nc.psum_tensor("ctx3", [65, 512], F32) as ctx3,
                  nc.psum_tensor("psS0", [128, 512], F32) as psS0,
                  nc.psum_tensor("psS1", [128, 512], F32) as psS1,
                  nc.psum_tensor("prl", [128, 512], F32) as prl,
                  nc.psum_tensor("pmt", [1, 128], F32) as pmt):
                ctxps = [ctx0, ctx1, ctx2, ctx3]
                psSs = [psS0, psS1]
                vb_free = [0, 0]; vb_loaded = [0, 0]
                kh_free = [[0, 0] for _ in range(4)]
                kh_loaded = [[0, 0] for _ in range(4)]
                psS_free = [0, 0]; pexp_free = [0, 0]

                def ld_vblk(kb, buf):
                    P.wait("sync", sT, vb_free[buf])
                    P.op("sync", lambda s, kb=kb, buf=buf: s.dma_start(
                        out=vblk[buf][:].rearrange("p (i c) -> p i c", i=8),
                        in_=v0sp[kb * 8:(kb + 1) * 8].rearrange("i p c -> p i c")), sld, 16)
                    vb_loaded[buf] = sld.n

                def ld_kh(hg, h4, kb, buf):
                    h = hg * 4 + h4
                    P.wait("sync", sT, kh_free[h4][buf])
                    P.op("sync", lambda s, h=h, kb=kb, h4=h4, buf=buf: s.dma_start(
                        out=khb[h4][buf][0:64, :],
                        in_=k0sp[h, :, kb * 1024:(kb + 1) * 1024]), sld, 16)
                    kh_loaded[h4][buf] = sld.n

                cnt = 0
                for hg in range(4):
                    for qc in range(4):
                        for h4 in range(4):
                            h = hg * 4 + h4
                            P.wait("vector", sT, sT.n)
                            P.op("vector", lambda v, h=h, h4=h4, qc=qc: v.tensor_copy(
                                qaug[h4][0:64, :],
                                qfm_sb[(h % 2) * 64:(h % 2) * 64 + 64,
                                       (h // 2) * TOWN + qc * 512:(h // 2) * TOWN + (qc + 1) * 512]),
                                sV)
                            for qtl in range(4):
                                qt = qc * 4 + qtl
                                P.wait("tensor", sV, sV.n)
                                P.op("tensor", lambda t, qt=qt, h=h: t.transpose(
                                    pmt[:], negm_sb[:, qt * 16 + h: qt * 16 + h + 1], identf[:]), sT)
                                P.wait("vector", sT, sT.n)
                                P.op("vector", lambda v, h4=h4, qtl=qtl: v.tensor_copy(
                                    qaug[h4][64:65, qtl * 128:(qtl + 1) * 128], pmt[:]), sV)
                        P.vsync(sV)
                        qaug_ready = sV.n
                        P.wait("tensor", sV, qaug_ready)
                        ctx_cnt = [0, 0, 0, 0]
                        pend = None
                        for kb in range(8):
                            buf = kb % 2
                            if hg == 0 and qc == 0 and kb == 0:
                                ld_vblk(0, 0)
                                for h4 in range(4):
                                    ld_kh(hg, h4, 0, 0)
                            if kb < 7:
                                ld_vblk(kb + 1, 1 - buf)
                                for h4 in range(4):
                                    ld_kh(hg, h4, kb + 1, 1 - buf)
                            elif qc < 3 or hg < 3:
                                nhg, nqc = (hg, qc + 1) if qc < 3 else (hg + 1, 0)
                                ld_vblk(0, 1 - buf)
                                for h4 in range(4):
                                    ld_kh(nhg, h4, 0, 1 - buf)
                            P.wait("tensor", sld, vb_loaded[buf])
                            for h4 in range(4):
                                P.wait("tensor", sld, kh_loaded[h4][buf])

                            def emit_ctx(item):
                                h, h4, kt, kbuf_, rot, e_done = item
                                P.wait("tensor", sS, e_done)
                                st_ = (ctx_cnt[h4] == 0)
                                sp_ = (ctx_cnt[h4] == 63)
                                P.op("tensor", lambda t, h=h, h4=h4, kt=kt, kbuf_=kbuf_,
                                     rot=rot, st_=st_, sp_=sp_: t.matmul(
                                    ctxps[h4][:],
                                    vblk[kbuf_][:, kt * 1040 + h * 65: kt * 1040 + h * 65 + 65],
                                    pexp[rot][:], start=st_, stop=sp_), sT)
                                ctx_cnt[h4] += 1
                                pexp_free[rot] = sT.n

                            for kt in range(8):
                                for h4 in range(4):
                                    h = hg * 4 + h4
                                    rot = cnt % 2
                                    P.wait("tensor", sS, psS_free[rot])
                                    P.op("tensor", lambda t, h4=h4, kt=kt, buf=buf, rot=rot: t.matmul(
                                        psSs[rot][:],
                                        khb[h4][buf][:, kt * 128:(kt + 1) * 128],
                                        qaug[h4][:], start=True, stop=True), sT)
                                    s_done = sT.n
                                    P.wait("scalar", sT, s_done)
                                    P.wait("scalar", sT, pexp_free[rot])
                                    P.op("scalar", lambda s, rot=rot: s.activation(
                                        pexp[rot][:], psSs[rot][:], AF.Exp), sS)
                                    psS_free[rot] = sS.n
                                    item = (h, h4, kt, buf, rot, sS.n)
                                    if pend is not None:
                                        emit_ctx(pend)
                                    pend = item
                                    cnt += 1
                            if pend is not None:
                                emit_ctx(pend)
                                pend = None
                            vb_free[buf] = sT.n
                            for h4 in range(4):
                                kh_free[h4][buf] = sT.n
                        for h4 in range(4):
                            h = hg * 4 + h4
                            P.wait("vector", sT, sT.n)
                            P.op("vector", lambda v, h4=h4: v.tensor_copy(
                                l_sb[:], ctxps[h4][64:65, :]), sV)
                            P.vdrain()
                            P.op("vector", lambda v: v.reciprocal(rl_sb[:], l_sb[:]), sV)
                            P.op("vector", lambda v, h4=h4: v.tensor_copy(
                                ctmp[0:64, :], ctxps[h4][0:64, :]), sV)
                            P.vsync(sV)
                            rl_ready = sV.n
                            P.wait("tensor", sV, rl_ready)
                            P.op("tensor", lambda t: t.matmul(
                                prl[:], onesr_f[0:1, 0:128], rl_sb[:],
                                start=True, stop=True), sT)
                            P.wait("vector", sT, sT.n)
                            dst = ctxf_sb[(h % 2) * 64:(h % 2) * 64 + 64,
                                          (h // 2) * TOWN + qc * 512:(h // 2) * TOWN + (qc + 1) * 512]
                            P.op("vector", lambda v, dst=dst: v.tensor_tensor(
                                dst, ctmp[0:64, :], prl[0:64, :], op=ALU.mult), sV)
                        P.wait("tensor", sV, sV.n)
                P.barrier(bsem)
                if debug:
                    P.op("sync", lambda s: s.dma_start(out=dbg["d_ctxf"], in_=ctxf_sb[:]), sO, 16)
                    P.barrier(bsem)

            ph2b = ExitStack()
            sb2b = lambda *a, **k: ph2b.enter_context(nc.sbuf_tensor(*a, **k))
            wproj_sb = sb2b("wproj_sb", [128, 8 * 1024], BF16)
            bproj_sb = sb2b("bproj_sb", [1, D], BF16)
            aout_sb = sb2b("aout_sb", [128, 1024], F32)
            P.op("gpsimd", lambda g: g.dma_start(out=wproj_sb[:].rearrange("p (k c) -> p k c", k=8),
                                                 in_=wproj.rearrange("(k p) c -> p k c", p=128)), sw, 16)
            P.op("gpsimd", lambda g: g.dma_start(out=bproj_sb[:], in_=bproj_row), sw, 16)
            with (nc.psum_tensor("pp0", [128, 512], F32) as pp0,
                  nc.psum_tensor("pp1", [128, 512], F32) as pp1):
                pps = [pp0, pp1]
                pp_free = [0, 0]
                P.wait("tensor", sw, sw.n)
                for tt in range(16):
                    for half in range(2):
                        P.wait("tensor", sV, pp_free[half])
                        P.op("tensor", lambda t, half=half: t.matmul(
                            pps[half][:], onesr_b[0:1, 0:128],
                            bproj_sb[0:1, half * 512:(half + 1) * 512],
                            start=True, stop=False), sT)
                        for c in range(8):
                            P.op("tensor", lambda t, tt=tt, c=c, half=half: t.matmul(
                                pps[half][:],
                                ctxf_sb[:, c * TOWN + tt * 128: c * TOWN + (tt + 1) * 128],
                                wproj_sb[:, c * 1024 + half * 512: c * 1024 + (half + 1) * 512],
                                start=False, stop=(c == 7)), sT)
                        P.wait("vector", sT, sT.n)
                        P.wait("vector", ssp, ssp.n)
                        P.op("vector", lambda v, half=half: v.tensor_copy(
                            aout_sb[:, half * 512:(half + 1) * 512], pps[half][:]), sV)
                        pp_free[half] = sV.n
                    P.wait("sync", sV, sV.n)
                    P.op("sync", lambda s, tt=tt: s.dma_start(
                        out=attn_sp[tt * 128:(tt + 1) * 128, :], in_=aout_sb[:]), ssp, 16)
                P.barrier(bsem)
                if debug:
                    P.op("sync", lambda s: s.dma_start(out=dbg["d_attn"], in_=attn_sp), sO, 16)
                    P.barrier(bsem)

            ph2b.close()
            ph2.close()
            ph3 = ExitStack()
            sb3 = lambda *a, **k: ph3.enter_context(nc.sbuf_tensor(*a, **k))
            combf_sb = sb3("combf_sb", [15, TOWN], F32)
            sel_sb = sb3("sel_sb", [15, E * 128], F32)
            gelu_sb = sb3("gelu_sb", [128, 256], F32)
            ph3a = ExitStack()
            sb3a = lambda *a, **k: ph3a.enter_context(nc.sbuf_tensor(*a, **k))
            gatew_sb = sb3a("gatew_sb", [128, 8 * 16], BF16)
            glog_sb = sb3a("glog_sb", [15, TOWN], F32)
            gtm_sb = sb3a("gtm_sb", [128, 16], F32)
            gw_sb = sb3a("gw_sb", [128, 8], F32)
            m8_sb = sb3a("m8_sb", [128, 8], F32)
            mask_sb = sb3a("mask_sb", [128, 16], F32)
            P.op("gpsimd", lambda g: g.dma_start(
                out=gatew_sb[:].rearrange("p (k c) -> p k c", k=8)[:, :, 0:15],
                in_=gatewT.rearrange("(k p) c -> p k c", p=128)), sw, 16)
            P.op("sync", lambda s: s.dma_start(out=sel_sb[:], in_=sel), sld, 16)
            with (nc.psum_tensor("pg", [15, 512], F32) as pg,
                  nc.psum_tensor("pgt", [128, 16], F32) as pgt,
                  nc.psum_tensor("pct", [16, 128], F32) as pct):
                P.wait("tensor", sw, sw.n)
                P.wait("tensor", sld, sld.n)
                for qc in range(4):
                    for kc in range(8):
                        P.op("tensor", lambda t, qc=qc, kc=kc: t.matmul(
                            pg[:], gatew_sb[:, kc * 16: kc * 16 + 15],
                            xnf_sb[:, kc * TOWN + qc * 512: kc * TOWN + (qc + 1) * 512],
                            start=(kc == 0), stop=(kc == 7)), sT)
                    P.wait("vector", sT, sT.n)
                    P.op("vector", lambda v, qc=qc: v.tensor_copy(
                        glog_sb[:, qc * 512:(qc + 1) * 512], pg[:]), sV)
                    P.vsync(sV)
                    P.wait("tensor", sV, sV.n)
                for tt in range(16):
                    P.op("tensor", lambda t, tt=tt: t.transpose(
                        pgt[:, 0:15], glog_sb[:, tt * 128:(tt + 1) * 128],
                        identf[0:15, 0:15]), sT)
                    P.wait("vector", sT, sT.n)
                    P.op("vector", lambda v: v.tensor_copy(gtm_sb[:, 0:15], pgt[:, 0:15]), sV)
                    P.vdrain()
                    P.op("vector", lambda v: v.reduce_max(
                        gw_sb[:, 0:1], gtm_sb[:, 0:15], axis=AX.X, negate=True), sV)
                    P.wait("scalar", sV, sV.n)
                    P.op("scalar", lambda s: s.activation(
                        gtm_sb[:, 0:15], gtm_sb[:, 0:15], AF.Exp, bias=gw_sb[:, 0:1]), sS)
                    P.wait("vector", sS, sS.n)
                    P.op("vector", lambda v: v.reduce_sum(gw_sb[:, 1:2], gtm_sb[:, 0:15], axis=AX.X), sV)
                    P.vdrain()
                    P.op("vector", lambda v: v.reciprocal(gw_sb[:, 2:3], gw_sb[:, 1:2]), sV)
                    P.vdrain()
                    P.op("vector", lambda v: v.tensor_scalar(
                        gtm_sb[:, 0:15], gtm_sb[:, 0:15], gw_sb[:, 2:3], None, op0=ALU.mult), sV)
                    P.vdrain()
                    P.op("vector", lambda v: v.max(out=m8_sb[:], in_=gtm_sb[:, 0:15]), sV)
                    P.op("vector", lambda v: v.memset(m8_sb[:, 3:8], NEG), smr)
                    P.wait("vector", smr, smr.n)
                    P.op("vector", lambda v: v.match_replace(
                        out=mask_sb[:, 0:15], in_to_replace=m8_sb[:], in_values=gtm_sb[:, 0:15],
                        imm_value=0.0), sV)
                    P.vdrain()
                    P.op("vector", lambda v: v.tensor_sub(
                        mask_sb[:, 0:15], gtm_sb[:, 0:15], mask_sb[:, 0:15]), sV)
                    P.op("vector", lambda v: v.reduce_sum(gw_sb[:, 3:4], m8_sb[:, 0:3], axis=AX.X), sV)
                    P.vdrain()
                    P.op("vector", lambda v: v.tensor_scalar(
                        gw_sb[:, 3:4], gw_sb[:, 3:4], 1e-20, None, op0=ALU.add), sV)
                    P.vdrain()
                    P.op("vector", lambda v: v.reciprocal(gw_sb[:, 4:5], gw_sb[:, 3:4]), sV)
                    P.vdrain()
                    P.op("vector", lambda v: v.tensor_scalar(
                        mask_sb[:, 0:15], mask_sb[:, 0:15], gw_sb[:, 4:5], None, op0=ALU.mult), sV)
                    P.op("vector", lambda v: v.memset(mask_sb[:, 15:16], 0.0), sV)
                    P.vsync(sV)
                    P.wait("tensor", sV, sV.n)
                    P.op("tensor", lambda t: t.transpose(pct[:], mask_sb[:, 0:16], identf[:]), sT)
                    P.wait("vector", sT, sT.n)
                    P.op("vector", lambda v, tt=tt: v.tensor_copy(
                        combf_sb[:, tt * 128:(tt + 1) * 128], pct[0:15, :]), sV)
                    P.vsync(sV)
                    P.wait("tensor", sV, sV.n)
                P.barrier(bsem)
                if debug:
                    P.op("sync", lambda s: s.dma_start(out=dbg["d_comb"], in_=combf_sb[:]), sO, 16)
                    P.barrier(bsem)

            ph3a.close()
            ph3b = ExitStack()
            sb3b = lambda *a, **k: ph3b.enter_context(nc.sbuf_tensor(*a, **k))
            w1buf = [sb3b(f"w1buf{i}", [128, 8 * 2048], BF16) for i in range(2)]
            w2buf = [sb3b(f"w2buf{i}", [128, 8 * 1024], BF16) for i in range(2)]
            fc1bx_sb = sb3b("fc1bx_sb", [128, E * 8], F32)
            fc1bg_sb = sb3b("fc1bg_sb", [128, E * 8], F32)
            multc_sb = sb3b("multc_sb", [128, E * 8], F32)
            fc2b_sb = sb3b("fc2b_sb", [15, D], F32)
            cbc_sb = sb3b("cbc_sb", [128, 256], F32)
            cbc2_sb = sb3b("cbc2_sb", [128, 256], F32)
            af_sb = sb3b("af_sb", [128, 8 * 256], BF16)
            ymoe_sb = sb3b("ymoe_sb", [128, 1024], F32)
            P.op("sync", lambda s: s.dma_start(out=fc1bx_sb[:], in_=fc1b_xcols), sld, 16)
            P.op("sync", lambda s: s.dma_start(out=fc1bg_sb[:], in_=fc1b_gcols), sld, 16)
            P.op("sync", lambda s: s.dma_start(out=multc_sb[:], in_=mult_cols), sld, 16)
            P.op("sync", lambda s: s.dma_start(out=fc2b_sb[:], in_=fc2b), sld, 16)
            ld0 = sld.n
            w1_loaded = [0, 0]; w2_loaded = [0, 0]
            w1_free = [0, 0]; w2_free = [0, 0]

            def ld_w12(e, buf):
                P.wait("gpsimd", sT, w1_free[buf])
                P.op("gpsimd", lambda g, e=e, buf=buf: g.dma_start(
                    out=w1buf[buf][:].rearrange("p (k c) -> p k c", k=8),
                    in_=fc1wT[e].rearrange("(k p) c -> p k c", p=128)), sw, 16)
                w1_loaded[buf] = sw.n
                P.wait("gpsimd", sT, w2_free[buf])
                P.op("gpsimd", lambda g, e=e, buf=buf: g.dma_start(
                    out=w2buf[buf][:].rearrange("p (k c) -> p k c", k=8),
                    in_=fc2wT[e].rearrange("(k p) c -> p k c", p=128)), sw, 16)
                w2_loaded[buf] = sw.n

            with (nc.psum_tensor("psAB", [128, 512], F32) as psAB,
                  nc.psum_tensor("pcb", [128, 256], F32) as pcb,
                  nc.psum_tensor("yp0", [128, 1024], F32) as yp0,
                  nc.psum_tensor("yp1", [128, 1024], F32) as yp1):
                yps = [yp0, yp1]
                ecnt = 0
                ld_w12(0, 0)
                P.wait("tensor", sld, ld0)
                for qc2 in range(8):
                    for e in range(E):
                        buf = ecnt % 2
                        if ecnt + 1 < 8 * E:
                            ne = (e + 1) % E
                            ld_w12(ne, 1 - buf)
                        P.wait("tensor", sw, w1_loaded[buf])
                        P.wait("tensor", sw, w2_loaded[buf])
                        P.wait("tensor", sV, sV.n)
                        P.op("tensor", lambda t, e=e, qc2=qc2: t.matmul(
                            pcb[:], sel_sb[:, e * 128:(e + 1) * 128],
                            combf_sb[:, qc2 * 256:(qc2 + 1) * 256],
                            start=True, stop=True), sT)
                        P.wait("vector", sT, sT.n)
                        P.op("vector", lambda v: v.tensor_copy(cbc_sb[:], pcb[:]), sV)
                        P.vdrain()
                        cbc_ready = sV.n
                        for j in range(8):
                            P.wait("tensor", sS, sS.n)
                            P.wait("tensor", sV, sV.n)
                            for kc in range(8):
                                P.op("tensor", lambda t, j=j, kc=kc, qc2=qc2, buf=buf: t.matmul(
                                    psAB[:, 0:256],
                                    w1buf[buf][:, kc * 2048 + j * 128: kc * 2048 + (j + 1) * 128],
                                    xnf_sb[:, kc * TOWN + qc2 * 256: kc * TOWN + (qc2 + 1) * 256],
                                    start=(kc == 0), stop=(kc == 7)), sT)
                            for kc in range(8):
                                P.op("tensor", lambda t, j=j, kc=kc, qc2=qc2, buf=buf: t.matmul(
                                    psAB[:, 256:512],
                                    w1buf[buf][:, kc * 2048 + 1024 + j * 128:
                                               kc * 2048 + 1024 + (j + 1) * 128],
                                    xnf_sb[:, kc * TOWN + qc2 * 256: kc * TOWN + (qc2 + 1) * 256],
                                    start=(kc == 0), stop=(kc == 7)), sT)
                            fc1_done = sT.n
                            P.wait("scalar", sT, fc1_done)
                            P.op("scalar", lambda s, e=e, j=j: s.activation(
                                gelu_sb[:], psAB[:, 256:512], AF.Gelu,
                                bias=fc1bg_sb[:, e * 8 + j: e * 8 + j + 1]), sS)
                            P.wait("vector", sS, sS.n)
                            P.wait("vector", sV, cbc_ready)
                            P.op("vector", lambda v, e=e, j=j: v.tensor_scalar(
                                cbc2_sb[:], psAB[:, 0:256],
                                fc1bx_sb[:, e * 8 + j: e * 8 + j + 1],
                                multc_sb[:, e * 8 + j: e * 8 + j + 1],
                                op0=ALU.add, op1=ALU.mult), sV)
                            P.vdrain()
                            P.op("vector", lambda v, j=j: v.tensor_mul(
                                cbc2_sb[:], cbc2_sb[:], gelu_sb[:]), sV)
                            P.vdrain()
                            P.op("vector", lambda v, j=j: v.tensor_mul(
                                af_sb[:, j * 256:(j + 1) * 256], cbc2_sb[:], cbc_sb[:]), sV)
                        P.vsync(sV)
                        af_ready = sV.n
                        P.wait("tensor", sV, af_ready)
                        for tt2 in range(2):
                            for half in range(2):
                                for j in range(8):
                                    P.op("tensor", lambda t, e=e, j=j, tt2=tt2, half=half,
                                         buf=buf: t.matmul(
                                        yps[tt2][:, half * 512:(half + 1) * 512],
                                        af_sb[:, j * 256 + tt2 * 128: j * 256 + tt2 * 128 + 128],
                                        w2buf[buf][:, j * 1024 + half * 512:
                                                   j * 1024 + (half + 1) * 512],
                                        start=(e == 0 and j == 0), stop=False), sT)
                        w1_free[buf] = sT.n
                        w2_free[buf] = sT.n
                        ecnt += 1
                    for tt2 in range(2):
                        tg = qc2 * 256 + tt2 * 128
                        for half in range(2):
                            P.op("tensor", lambda t, tg=tg, half=half, tt2=tt2: t.matmul(
                                yps[tt2][:, half * 512:(half + 1) * 512],
                                combf_sb[:, tg:tg + 128],
                                fc2b_sb[:, half * 512:(half + 1) * 512],
                                start=False, stop=True), sT)
                        P.wait("vector", sT, sT.n)
                        P.wait("vector", ssp, ssp.n)
                        P.op("vector", lambda v, tt2=tt2: v.tensor_copy(ymoe_sb[:], yps[tt2][:]), sV)
                        P.wait("sync", sV, sV.n)
                        P.op("sync", lambda s, tg=tg: s.dma_start(
                            out=y_sp[tg:tg + 128, :], in_=ymoe_sb[:]), ssp, 16)
                    P.wait("tensor", ssp, ssp.n)
                P.barrier(bsem)
                if debug:
                    P.op("sync", lambda s: s.dma_start(out=dbg["d_ymoe"], in_=y_sp), sO, 16)
                    P.barrier(bsem)

            ph3b.close()
            ph3c = ExitStack()
            sb3c = lambda *a, **k: ph3c.enter_context(nc.sbuf_tensor(*a, **k))
            sw1buf = [sb3c(f"sw1buf{i}", [128, 2048], BF16) for i in range(2)]
            sw2buf = [sb3c(f"sw2buf{i}", [128, 1024], BF16) for i in range(2)]
            sb1x_sb = sb3c("sb1x_sb", [128, 16], F32)
            sb1g_sb = sb3c("sb1g_sb", [128, 16], F32)
            smultc_sb = sb3c("smultc_sb", [128, 16], F32)
            sb2r_sb = sb3c("sb2r_sb", [1, D], BF16)
            asf_sb = sb3c("asf_sb", [128, 16 * 256], BF16)
            scr_sb = sb3c("scr_sb", [128, 256], F32)
            xa_sb = sb3c("xa_sb", [128, 1024], F32)
            xb2_sb = sb3c("xb2_sb", [128, 1024], F32)
            x1_sb = sb3c("x1_sb", [128, 1024], F32)
            P.op("sync", lambda s: s.dma_start(out=sb1x_sb[:], in_=sfc1b_xcols), sld, 16)
            P.op("sync", lambda s: s.dma_start(out=sb1g_sb[:], in_=sfc1b_gcols), sld, 16)
            P.op("sync", lambda s: s.dma_start(out=smultc_sb[:], in_=smult_cols), sld, 16)
            P.op("gpsimd", lambda g: g.dma_start(out=sb2r_sb[:], in_=sfc2b_row), sw, 16)
            ldc = sld.n; swc = sw.n
            sw1_loaded = [0, 0]; sw1_free = [0, 0]
            sw2_loaded = [0, 0]; sw2_free = [0, 0]

            def ld_sw1(j2, buf):
                P.wait("gpsimd", sT, sw1_free[buf])
                P.op("gpsimd", lambda g, j2=j2, buf=buf: g.dma_start(
                    out=sw1buf[buf][:, 0:1024].rearrange("p (k c) -> p k c", k=8),
                    in_=sfc1wT[:, j2 * 128:(j2 + 1) * 128].rearrange("(k p) c -> p k c", p=128)),
                    sw, 16)
                P.op("gpsimd", lambda g, j2=j2, buf=buf: g.dma_start(
                    out=sw1buf[buf][:, 1024:2048].rearrange("p (k c) -> p k c", k=8),
                    in_=sfc1wT[:, FS + j2 * 128: FS + (j2 + 1) * 128].rearrange("(k p) c -> p k c", p=128)),
                    sw, 16)
                sw1_loaded[buf] = sw.n

            def ld_sw2(j2, buf):
                P.wait("gpsimd", sT, sw2_free[buf])
                P.op("gpsimd", lambda g, j2=j2, buf=buf: g.dma_start(
                    out=sw2buf[buf][:], in_=sfc2wT[j2 * 128:(j2 + 1) * 128, :]), sw, 16)
                sw2_loaded[buf] = sw.n

            with (nc.psum_tensor("psAB2", [128, 512], F32) as psAB2,
                  nc.psum_tensor("ypa", [128, 1024], F32) as ypa,
                  nc.psum_tensor("ypb", [128, 1024], F32) as ypb):
                yps2 = [ypa, ypb]
                P.wait("tensor", sld, ldc)
                P.wait("tensor", sw, swc)
                ld_sw1(0, 0)
                for qc2 in range(8):
                    for j2 in range(16):
                        buf = j2 % 2
                        if qc2 * 16 + j2 + 1 < 8 * 16:
                            ld_sw1((j2 + 1) % 16, 1 - buf)
                        P.wait("tensor", sw, sw1_loaded[buf])
                        P.wait("tensor", sS, sS.n)
                        P.wait("tensor", sV, sV.n)
                        for kc in range(8):
                            P.op("tensor", lambda t, kc=kc, qc2=qc2, buf=buf: t.matmul(
                                psAB2[:, 0:256], sw1buf[buf][:, kc * 128:(kc + 1) * 128],
                                xnf_sb[:, kc * TOWN + qc2 * 256: kc * TOWN + (qc2 + 1) * 256],
                                start=(kc == 0), stop=(kc == 7)), sT)
                        for kc in range(8):
                            P.op("tensor", lambda t, kc=kc, qc2=qc2, buf=buf: t.matmul(
                                psAB2[:, 256:512], sw1buf[buf][:, 1024 + kc * 128: 1024 + (kc + 1) * 128],
                                xnf_sb[:, kc * TOWN + qc2 * 256: kc * TOWN + (qc2 + 1) * 256],
                                start=(kc == 0), stop=(kc == 7)), sT)
                        sfc1_done = sT.n
                        sw1_free[buf] = sT.n
                        P.wait("scalar", sT, sfc1_done)
                        P.op("scalar", lambda s, j2=j2: s.activation(
                            gelu_sb[:], psAB2[:, 256:512], AF.Gelu, bias=sb1g_sb[:, j2:j2 + 1]), sS)
                        P.wait("vector", sS, sS.n)
                        P.op("vector", lambda v, j2=j2: v.tensor_scalar(
                            scr_sb[:], psAB2[:, 0:256], sb1x_sb[:, j2:j2 + 1],
                            smultc_sb[:, j2:j2 + 1], op0=ALU.add, op1=ALU.mult), sV)
                        P.vdrain()
                        P.op("vector", lambda v, j2=j2: v.tensor_mul(
                            asf_sb[:, j2 * 256:(j2 + 1) * 256], scr_sb[:], gelu_sb[:]), sV)
                    P.vsync(sV)
                    asf_ready = sV.n
                    P.wait("tensor", sV, asf_ready)
                    for tt2 in range(2):
                        for half in range(2):
                            P.op("tensor", lambda t, tt2=tt2, half=half: t.matmul(
                                yps2[tt2][:, half * 512:(half + 1) * 512],
                                onesr_b[0:1, 0:128], sb2r_sb[0:1, half * 512:(half + 1) * 512],
                                start=True, stop=False), sT)
                    jcnt0 = qc2 * 16
                    if qc2 == 0:
                        ld_sw2(0, 0)
                    for j2 in range(16):
                        buf = j2 % 2
                        if qc2 * 16 + j2 + 1 < 8 * 16:
                            ld_sw2((j2 + 1) % 16, 1 - buf)
                        P.wait("tensor", sw, sw2_loaded[buf])
                        for tt2 in range(2):
                            for half in range(2):
                                P.op("tensor", lambda t, j2=j2, tt2=tt2, half=half, buf=buf: t.matmul(
                                    yps2[tt2][:, half * 512:(half + 1) * 512],
                                    asf_sb[:, j2 * 256 + tt2 * 128: j2 * 256 + tt2 * 128 + 128],
                                    sw2buf[buf][:, half * 512:(half + 1) * 512],
                                    start=False, stop=(j2 == 15)), sT)
                        sw2_free[buf] = sT.n
                    sfc2_done = sT.n
                    for tt2 in range(2):
                        tg = qc2 * 256 + tt2 * 128
                        P.wait("sync", sV, sV.n)
                        P.op("sync", lambda s, tg=tg: s.dma_start(
                            out=xa_sb[:], in_=xr[tg:tg + 128, :]), sld, 16)
                        P.op("sync", lambda s, tg=tg: s.dma_start(
                            out=xb2_sb[:], in_=attn_sp[tg:tg + 128, :]), sld, 16)
                        P.wait("vector", sld, sld.n)
                        P.wait("vector", sT, sfc2_done)
                        P.op("vector", lambda v: v.tensor_add(x1_sb[:], xa_sb[:], xb2_sb[:]), sV)
                        P.vdrain()
                        P.wait("sync", sV, sV.n)
                        P.op("sync", lambda s, tg=tg: s.dma_start(
                            out=xa_sb[:], in_=y_sp[tg:tg + 128, :]), sld, 16)
                        P.wait("vector", sld, sld.n)
                        P.op("vector", lambda v: v.tensor_add(x1_sb[:], x1_sb[:], xa_sb[:]), sV)
                        P.vdrain()
                        P.op("vector", lambda v, tt2=tt2: v.tensor_add(
                            x1_sb[:], x1_sb[:], yps2[tt2][:]), sV)
                        P.wait("sync", sV, sV.n)
                        P.op("sync", lambda s, tg=tg: s.dma_start(
                            out=x1own[tg:tg + 128, :], in_=x1_sb[:]), sO, 16)
                        P.wait("vector", sO, sO.n)
                    P.wait("tensor", sO, sO.n)
            ph3c.close()
            ph3.close()
            P.wait("sync", sO, sO.n)
            P.wait("sync", ssp, ssp.n)
            P.emit(block)
    return nc



def build_l1(debug=False):
    nc = bass.Bass("TRN2", target_bir_lowering=False)
    xr = nc.dram_tensor("xr", [TALL, D], F32, kind="ExternalInput").ap()
    wqk = nc.dram_tensor("wqk", [D, 128], F32, kind="ExternalInput").ap()
    wv = nc.dram_tensor("wv", [D, D], F32, kind="ExternalInput").ap()
    w1 = nc.dram_tensor("w1", [D, 4096], F32, kind="ExternalInput").ap()
    b1_cols = nc.dram_tensor("b1_cols", [128, 32], F32, kind="ExternalInput").ap()
    w2 = nc.dram_tensor("w2", [4096, D], F32, kind="ExternalInput").ap()
    b2_row = nc.dram_tensor("b2_row", [1, D], F32, kind="ExternalInput").ap()
    iden = nc.dram_tensor("iden", [128, 128], F32, kind="ExternalInput").ap()
    maskval = nc.dram_tensor("maskval", [1, TALL], F32, kind="ExternalInput").ap()
    biascols = nc.dram_tensor("biascols", [128, 8 * 64], F32, kind="ExternalInput").ap()
    diagA = nc.dram_tensor("diagA", [128, 4 * 512], F32, kind="ExternalInput").ap()
    diagB = nc.dram_tensor("diagB", [128, 2 * 256], F32, kind="ExternalInput").ap()
    out_own = nc.dram_tensor("out_own", [TOWN, D], F32, kind="ExternalOutput").ap()
    dbg = {}
    if debug:
        for nm, shp, dt in (("d_negm", [128, 16], F32), ("d_att", [TOWN, D], F32),
                            ("d_xnf", [128, 16384], BF16), ("d_k", [65, TALL], BF16)):
            dbg[nm] = nc.dram_tensor(nm, shp, dt, kind="ExternalOutput").ap()
    v1sp = nc.dram_tensor("v1sp", [64, 128, 1024], BF16, kind="Internal").ap()
    att1_sp = nc.dram_tensor("att1_sp", [TOWN, D], F32, kind="Internal").ap()

    with ExitStack() as stack:
        P = Prog(nc, stack)
        sb = lambda *a, **k: stack.enter_context(nc.sbuf_tensor(*a, **k))
        sld = P.sem("sld"); sw = P.sem("sw"); ssp = P.sem("ssp")
        sT = P.sem("sT"); sV = P.sem("sV"); sS = P.sem("sS"); sO = P.sem("sO")
        bsem = {e: P.sem("b_" + e) for e in ENGS}

        identb = sb("identb", [128, 128], BF16)
        identf = sb("identf", [128, 128], F32)
        onesc_b = sb("onesc_b", [128, 1], BF16)
        onesr_b = sb("onesr_b", [1, 512], BF16)
        eps_sb = sb("eps_sb", [128, 1], F32)
        xn1f_sb = sb("xn1f_sb", [128, 8 * TOWN], BF16)
        q1f_sb = sb("q1f_sb", [64, TOWN], BF16)
        k1a_sb = sb("k1a_sb", [65, TALL], BF16)
        k1b_sb = sb("k1b_sb", [65, TALL], BF16)
        negm_sb = sb("negm_sb", [128, 16], F32)
        st_sb = sb("st_sb", [128, 64], F32)
        diagA_sb = sb("diagA_sb", [128, 4 * 512], F32)
        diagB_sb = sb("diagB_sb", [128, 2 * 256], BF16)
        bias_sb = sb("bias_sb", [128, 8 * 64], F32)
        l1ph0 = ExitStack()
        sbA = lambda *a, **k: l1ph0.enter_context(nc.sbuf_tensor(*a, **k))
        wqk_sb = sbA("wqk_sb", [128, 8 * 128], BF16)
        wv_sb = sbA("wv_sb", [128, 8 * 1024], BF16)
        x_sb = [sbA(f"x_sb{i}", [128, 4096], F32) for i in range(2)]
        v2_sb = sbA("v2_sb", [128, 4096], BF16)
        sq_sb = sbA("sq_sb", [128, 1024], F32)
        xnf_blk = sbA("xnf_blk", [128, 8 * 512], BF16)
        vtile = [sbA(f"vtile{i}", [128, 1024], BF16) for i in range(2)]
        C = lambda j, i: st_sb[:, j + i:j + i + 1]

        with nc.Block() as block:
            P.op("gpsimd", lambda g: g.dma_start(out=wqk_sb[:].rearrange("p (k c) -> p k c", k=8),
                                                 in_=wqk.rearrange("(k p) c -> p k c", p=128)), sw, 16)
            P.op("gpsimd", lambda g: g.dma_start(out=wv_sb[:].rearrange("p (k c) -> p k c", k=8),
                                                 in_=wv.rearrange("(k p) c -> p k c", p=128)), sw, 16)
            P.op("gpsimd", lambda g: g.dma_start(out=identb[:], in_=iden), sw, 16)
            P.op("sync", lambda s: s.dma_start(out=identf[:], in_=iden), sld, 16)
            P.op("gpsimd", lambda g: g.dma_start(out=k1a_sb[64:65, :], in_=maskval), sw, 16)
            P.op("sync", lambda s: s.dma_start(out=diagA_sb[:], in_=diagA), sld, 16)
            P.op("gpsimd", lambda g: g.dma_start(out=diagB_sb[:], in_=diagB), sw, 16)
            P.op("sync", lambda s: s.dma_start(out=bias_sb[:], in_=biascols), sld, 16)
            P.op("vector", lambda v: v.memset(onesc_b[:], 1.0), sV)
            P.op("vector", lambda v: v.memset(onesr_b[:], 1.0), sV)
            P.op("vector", lambda v: v.memset(eps_sb[:], 1e-5), sV)
            P.op("vector", lambda v: v.memset(k1b_sb[64:65, :], 1.0), sV)
            w_ready = sw.n
            ld_ready = sld.n

            with (nc.psum_tensor("pt0", [128, 1024], BF16) as pt0,
                  nc.psum_tensor("pt1", [128, 1024], BF16) as pt1,
                  nc.psum_tensor("pq0", [128, 512], F32) as pq0,
                  nc.psum_tensor("pv0", [128, 512], F32) as pv0,
                  nc.psum_tensor("pv1", [128, 512], F32) as pv1):
                pts = [pt0, pt1]; pvs = [pv0, pv1]
                x_free = [0, 0]; pt_free = [0, 0]; pv_free = [0, 0]; vt_free = [0, 0]
                x_loaded = [0, 0]

                def ld_x(tb, buf):
                    P.wait("sync", sV, x_free[buf])
                    P.op("sync", lambda s, tb=tb, buf=buf: s.dma_start(
                        out=x_sb[buf][:].rearrange("p (i d) -> p i d", i=4),
                        in_=xr[tb * 512:(tb + 1) * 512, :].rearrange("(i p) d -> p i d", p=128)),
                        sld, 16)
                    x_loaded[buf] = sld.n

                ld_x(0, 0)
                for tb in range(16):
                    buf = tb % 2
                    if tb < 15:
                        ld_x(tb + 1, 1 - buf)
                    xb = x_sb[buf]
                    own = tb < 4
                    P.wait("scalar", sld, x_loaded[buf])
                    P.wait("vector", sld, x_loaded[buf])
                    for i in range(4):
                        P.op("scalar", lambda s, i=i, xb=xb: s.activation(
                            sq_sb[:], xb[:, i * 1024:(i + 1) * 1024], AF.Square,
                            accum_out=C(4, i)), sS)
                        P.op("vector", lambda v, i=i, xb=xb: v.reduce_sum(
                            C(0, i), xb[:, i * 1024:(i + 1) * 1024], axis=AX.X), sV)
                    P.wait("vector", sS, sS.n)
                    P.vdrain()
                    for i in range(4):
                        P.op("vector", lambda v, i=i: v.tensor_scalar(
                            C(8, i), C(0, i), 1.0 / D, None, op0=ALU.mult), sV)
                        P.op("vector", lambda v, i=i: v.tensor_scalar(
                            C(48, i), C(4, i), 1.0 / D, None, op0=ALU.mult), sV)
                    P.vdrain()
                    for i in range(4):
                        P.op("vector", lambda v, i=i: v.tensor_mul(C(56, i), C(8, i), C(8, i)), sV)
                    P.vdrain()
                    for i in range(4):
                        P.op("vector", lambda v, i=i: v.tensor_sub(C(48, i), C(48, i), C(56, i)), sV)
                    P.vsync(sV)
                    P.wait("scalar", sV, sV.n)
                    P.op("scalar", lambda s: s.activation(
                        st_sb[:, 52:56], st_sb[:, 48:52], AF.Sqrt, bias=eps_sb[:, 0:1]), sS)
                    P.wait("vector", sS, sS.n)
                    P.op("vector", lambda v: v.reciprocal(st_sb[:, 12:16], st_sb[:, 52:56]), sV)
                    P.vdrain()
                    for i in range(4):
                        P.op("vector", lambda v, i=i, xb=xb: v.tensor_scalar(
                            v2_sb[:, i * 1024:(i + 1) * 1024], xb[:, i * 1024:(i + 1) * 1024],
                            C(8, i), C(12, i), op0=ALU.subtract, op1=ALU.mult), sV)
                    x_free[buf] = sV.n
                    P.wait("tensor", sV, sV.n)
                    P.wait("tensor", sld, ld_ready)
                    for i in range(4):
                        for fc in range(8):
                            pi = (i * 8 + fc) % 2
                            P.wait("tensor", sV, pt_free[pi])
                            P.op("tensor", lambda t, i=i, fc=fc, pi=pi: t.transpose(
                                pts[pi][:, :128],
                                v2_sb[:, i * 1024 + fc * 128: i * 1024 + (fc + 1) * 128],
                                identb[:]), sT)
                            P.wait("vector", sT, sT.n)
                            P.op("vector", lambda v, i=i, fc=fc, pi=pi: v.tensor_copy(
                                xnf_blk[:, fc * 512 + i * 128: fc * 512 + (i + 1) * 128],
                                pts[pi][:, :128]), sV)
                            if own:
                                tg = tb * 512 + i * 128
                                P.op("vector", lambda v, fc=fc, tg=tg, pi=pi: v.tensor_copy(
                                    xn1f_sb[:, fc * TOWN + tg: fc * TOWN + tg + 128],
                                    pts[pi][:, :128]), sV)
                            pt_free[pi] = sV.n
                    P.vsync(sV)
                    blk_ready = sV.n
                    P.wait("tensor", sV, blk_ready)
                    P.wait("tensor", sw, w_ready)
                    for kc in range(8):
                        P.op("tensor", lambda t, kc=kc: t.matmul(
                            pq0[:], wqk_sb[:, kc * 128:(kc + 1) * 128],
                            xnf_blk[:, kc * 512:(kc + 1) * 512],
                            start=(kc == 0), stop=(kc == 7)), sT)
                    P.wait("vector", sT, sT.n)
                    if own:
                        P.op("vector", lambda v, tb=tb: v.tensor_scalar(
                            q1f_sb[:, tb * 512:(tb + 1) * 512], pq0[0:64, :], 0.125, None,
                            op0=ALU.mult), sV)
                    P.op("vector", lambda v, tb=tb: v.tensor_copy(
                        k1a_sb[0:64, tb * 512:(tb + 1) * 512], pq0[64:128, :]), sV)
                    P.op("vector", lambda v, tb=tb: v.tensor_copy(
                        k1b_sb[0:64, tb * 512:(tb + 1) * 512], pq0[64:128, :]), sV)
                    P.wait("tensor", sV, sV.n)
                    for i in range(4):
                        vi = i % 2
                        P.wait("vector", ssp, vt_free[vi])
                        for half in range(2):
                            P.wait("tensor", sV, pv_free[half])
                            for kc in range(8):
                                P.op("tensor", lambda t, i=i, kc=kc, half=half: t.matmul(
                                    pvs[half][:],
                                    xnf_blk[:, kc * 512 + i * 128: kc * 512 + (i + 1) * 128],
                                    wv_sb[:, kc * 1024 + half * 512: kc * 1024 + (half + 1) * 512],
                                    start=(kc == 0), stop=(kc == 7)), sT)
                            P.wait("vector", sT, sT.n)
                            P.op("vector", lambda v, half=half, vi=vi: v.tensor_copy(
                                vtile[vi][:, half * 512:(half + 1) * 512], pvs[half][:]), sV)
                            pv_free[half] = sV.n
                        P.wait("sync", sV, sV.n)
                        P.op("sync", lambda s, tb=tb, i=i, vi=vi: s.dma_start(
                            out=v1sp[tb * 4 + i], in_=vtile[vi][:]), ssp, 16)
                        vt_free[vi] = ssp.n
                P.barrier(bsem)
                if debug:
                    P.op("sync", lambda s: s.dma_start(out=dbg["d_xnf"], in_=xn1f_sb[:]), sO, 16)
                    P.op("sync", lambda s: s.dma_start(out=dbg["d_k"], in_=k1b_sb[:]), sO, 16)
                    P.barrier(bsem)

            q1aug = sbA("q1aug", [65, 128], BF16)
            P.op("vector", lambda v: v.memset(q1aug[64:65, :], 1.0), sV)
            with nc.psum_tensor("psA", [128, 2048], F32) as psA:
                for qt in range(16):
                    P.wait("vector", sT, sT.n)
                    P.op("vector", lambda v, qt=qt: v.tensor_copy(
                        q1aug[0:64, :], q1f_sb[:, qt * 128:(qt + 1) * 128]), sV)
                    P.vsync(sV)
                    P.wait("tensor", sV, sV.n)
                    first_col = negm_sb[:, qt:qt + 1]
                    tmp = st_sb[:, 63:64]
                    got = False
                    for grp in range(4):
                        kcs = (list(range(0, qt // 4 + 1)) if grp == 0
                               else list(range(grp * 4, grp * 4 + 4)))
                        for ci, kc in enumerate(kcs):
                            P.op("tensor", lambda t, kc=kc, ci=ci: t.matmul(
                                psA[:, ci * 512:(ci + 1) * 512], q1aug[:],
                                k1a_sb[:, kc * 512:(kc + 1) * 512],
                                start=True, stop=True), sT)
                        mm_done = sT.n
                        P.wait("vector", sT, mm_done)
                        if grp == 0:
                            ci = len(kcs) - 1
                            d = qt % 4
                            P.op("vector", lambda v, ci=ci, d=d: v.tensor_add(
                                psA[:, ci * 512:(ci + 1) * 512],
                                psA[:, ci * 512:(ci + 1) * 512],
                                diagA_sb[:, d * 512:(d + 1) * 512]), sV)
                        w = len(kcs) * 512
                        if not got:
                            P.op("vector", lambda v, w=w, first_col=first_col: v.reduce_max(
                                first_col, psA[:, 0:w], axis=AX.X, negate=True), sV)
                            got = True
                        else:
                            P.op("vector", lambda v, w=w, tmp=tmp: v.reduce_max(
                                tmp, psA[:, 0:w], axis=AX.X, negate=True), sV)
                            P.vdrain()
                            P.op("vector", lambda v, first_col=first_col, tmp=tmp: v.tensor_tensor(
                                first_col, first_col, tmp, op=ALU.min), sV)
                            P.vdrain()
                        P.wait("tensor", sV, sV.n)
                P.barrier(bsem)
                if debug:
                    P.op("sync", lambda s: s.dma_start(out=dbg["d_negm"], in_=negm_sb[:, 0:16]), sO, 16)
                    P.barrier(bsem)

            l1ph0.close()
            l1ph2 = ExitStack()
            sbB = lambda *a, **k: l1ph2.enter_context(nc.sbuf_tensor(*a, **k))
            qaug_b = sbB("qaug_b", [65, 256], BF16)
            vblk = [sbB(f"vblk{i}", [128, 8 * 1024], BF16) for i in range(2)]
            pexp = [sbB(f"pexp{i}", [128, 256], BF16) for i in range(2)]
            rl_sb = sbB("rl_sb", [128, 2], F32)
            x1t_sb = sbB("x1t_sb", [128, 1024], F32)
            at_sb = sbB("at_sb", [128, 1024], F32)
            with (nc.psum_tensor("yc0", [128, 1024], F32) as yc0,
                  nc.psum_tensor("yc1", [128, 1024], F32) as yc1,
                  nc.psum_tensor("lc0", [128, 1], F32) as lc0,
                  nc.psum_tensor("lc1", [128, 1], F32) as lc1,
                  nc.psum_tensor("psS0", [128, 256], F32) as psS0,
                  nc.psum_tensor("psS1", [128, 256], F32) as psS1):
                ycs = [yc0, yc1]; psSs = [psS0, psS1]; lcs = [lc0, lc1]
                vb_free = [0, 0]; vb_loaded = [0, 0]
                psS_free = [0, 0]; pexp_free = [0, 0]
                vld = [0]
                vuse = [0]

                def ld_vblk1(kb):
                    buf = vld[0] % 2
                    vld[0] += 1
                    P.wait("sync", sT, vb_free[buf])
                    P.op("sync", lambda s, kb=kb, buf=buf: s.dma_start(
                        out=vblk[buf][:].rearrange("p (i c) -> p i c", i=8),
                        in_=v1sp[kb * 8:(kb + 1) * 8].rearrange("i p c -> p i c")), sld, 16)
                    vb_loaded[buf] = sld.n

                cnt = 0
                for qc in range(8):
                    P.wait("vector", sT, sT.n)
                    P.op("vector", lambda v, qc=qc: v.tensor_copy(
                        qaug_b[0:64, :], q1f_sb[:, qc * 256:(qc + 1) * 256]), sV)
                    P.wait("tensor", sS, psS_free[0])
                    for qtl in range(2):
                        qt = qc * 2 + qtl
                        P.wait("tensor", sV, sV.n)
                        P.op("tensor", lambda t, qt=qt: t.transpose(
                            psS0[0:1, 0:128], negm_sb[:, qt:qt + 1], identf[:]), sT)
                        P.wait("vector", sT, sT.n)
                        P.op("vector", lambda v, qtl=qtl: v.tensor_copy(
                            qaug_b[64:65, qtl * 128:(qtl + 1) * 128], psS0[0:1, 0:128]), sV)
                    P.vsync(sV)
                    qa_ready = sV.n
                    P.wait("tensor", sV, qa_ready)
                    kts = list(range(0, qc * 2 + 2)) + list(range(16, 64))
                    kbs = sorted(set(kt // 8 for kt in kts))
                    if qc == 0:
                        ld_vblk1(kbs[0])
                    first = True
                    bi = 0
                    for kb in kbs:
                        buf = vuse[0] % 2
                        vuse[0] += 1
                        if bi + 1 < len(kbs):
                            ld_vblk1(kbs[bi + 1])
                        elif qc < 7:
                            ld_vblk1(0)
                        P.wait("tensor", sld, vb_loaded[buf])
                        for kt in [k for k in kts if k // 8 == kb]:
                            rot = cnt % 2
                            P.wait("tensor", sS, psS_free[rot])
                            P.op("tensor", lambda t, kt=kt, rot=rot: t.matmul(
                                psSs[rot][:], k1b_sb[:, kt * 128:(kt + 1) * 128],
                                qaug_b[:], start=True, stop=True), sT)
                            s_done = sT.n
                            P.wait("scalar", sT, s_done)
                            P.wait("scalar", sT, pexp_free[rot])
                            P.op("scalar", lambda s, kt=kt, qc=qc, rot=rot: s.activation(
                                pexp[rot][:], psSs[rot][:], AF.Exp,
                                bias=bias_sb[:, qc * 64 + kt: qc * 64 + kt + 1]), sS)
                            psS_free[rot] = sS.n
                            e_done = sS.n
                            if kt in (qc * 2, qc * 2 + 1):
                                dd = kt - qc * 2
                                P.wait("vector", sS, e_done)
                                P.op("vector", lambda v, rot=rot, dd=dd: v.tensor_mul(
                                    pexp[rot][:], pexp[rot][:],
                                    diagB_sb[:, dd * 256:(dd + 1) * 256]), sV)
                                P.wait("tensor", sV, sV.n)
                            else:
                                P.wait("tensor", sS, e_done)
                            last = (kt == 63)
                            for qt2 in range(2):
                                for half in range(2):
                                    P.op("tensor", lambda t, kt=kt, qt2=qt2, half=half, buf=buf,
                                         rot=rot, first=first, last=last: t.matmul(
                                        ycs[qt2][:, half * 512:(half + 1) * 512],
                                        pexp[rot][:, qt2 * 128:(qt2 + 1) * 128],
                                        vblk[buf][:, (kt % 8) * 1024 + half * 512:
                                                  (kt % 8) * 1024 + (half + 1) * 512],
                                        start=first, stop=last), sT)
                                P.op("tensor", lambda t, qt2=qt2, rot=rot, first=first,
                                     last=last: t.matmul(
                                    lcs[qt2][:], pexp[rot][:, qt2 * 128:(qt2 + 1) * 128],
                                    onesc_b[:], start=first, stop=last), sT)
                            pexp_free[rot] = sT.n
                            first = False
                            cnt += 1
                        vb_free[buf] = sT.n
                        bi += 1
                    ctx_done = sT.n
                    P.wait("vector", sT, ctx_done)
                    P.op("vector", lambda v: v.reciprocal(rl_sb[:, 0:1], lc0[:]), sV)
                    P.op("vector", lambda v: v.reciprocal(rl_sb[:, 1:2], lc1[:]), sV)
                    P.vdrain()
                    for qt2 in range(2):
                        tg = qc * 256 + qt2 * 128
                        P.wait("sync", sV, sV.n)
                        P.op("sync", lambda s, tg=tg: s.dma_start(
                            out=x1t_sb[:], in_=xr[tg:tg + 128, :]), sld, 16)
                        P.wait("vector", sld, sld.n)
                        P.op("vector", lambda v, qt2=qt2: v.scalar_tensor_tensor(
                            at_sb[:], ycs[qt2][:], rl_sb[:, qt2:qt2 + 1], x1t_sb[:],
                            op0=ALU.mult, op1=ALU.add), sV)
                        P.vdrain()
                        P.wait("sync", sV, sV.n)
                        P.op("sync", lambda s, tg=tg: s.dma_start(
                            out=att1_sp[tg:tg + 128, :], in_=at_sb[:]), ssp, 16)
                        P.wait("vector", ssp, ssp.n)
                    P.wait("tensor", ssp, ssp.n)
                P.barrier(bsem)
                if debug:
                    P.op("sync", lambda s: s.dma_start(out=dbg["d_att"], in_=att1_sp), sO, 16)
                    P.barrier(bsem)

            l1ph2.close()
            l1ph3 = ExitStack()
            sbC = lambda *a, **k: l1ph3.enter_context(nc.sbuf_tensor(*a, **k))
            w1jb = [sbC(f"w1jb{i}", [128, 8 * 512], BF16) for i in range(2)]
            w2buf = [sbC(f"w2buf{i}", [128, 1024], BF16) for i in range(2)]
            b1c_sb = sbC("b1c_sb", [128, 32], F32)
            b2r_sb = sbC("b2r_sb", [1, D], BF16)
            h1f_sb = sbC("h1f_sb", [128, 32 * 512], BF16)
            ff_sb = sbC("ff_sb", [128, 1024], F32)
            o_sb = sbC("o_sb", [128, 1024], F32)
            P.op("sync", lambda s: s.dma_start(out=b1c_sb[:], in_=b1_cols), sld, 16)
            P.op("gpsimd", lambda g: g.dma_start(out=b2r_sb[:], in_=b2_row), sw, 16)
            ldc = sld.n; swc = sw.n
            w1_loaded = [0, 0]; w1_free = [0, 0]
            w2_loaded = [0, 0]; w2_free = [0, 0]
            with (nc.psum_tensor("ph1", [128, 512], F32) as ph1,
                  nc.psum_tensor("yf0", [128, 1024], F32) as yf0,
                  nc.psum_tensor("yf1", [128, 1024], F32) as yf1):
                yfs = [yf0, yf1]

                def ld_w1j(jb, buf):
                    P.wait("gpsimd", sT, w1_free[buf])
                    P.op("gpsimd", lambda g, jb=jb, buf=buf: g.dma_start(
                        out=w1jb[buf][:].rearrange("p (k c) -> p k c", k=8),
                        in_=w1[:, jb * 512:(jb + 1) * 512].rearrange("(k p) c -> p k c", p=128)),
                        sw, 16)
                    w1_loaded[buf] = sw.n

                def ld_w2j(j, buf):
                    P.wait("gpsimd", sT, w2_free[buf])
                    P.op("gpsimd", lambda g, j=j, buf=buf: g.dma_start(
                        out=w2buf[buf][:], in_=w2[j * 128:(j + 1) * 128, :]), sw, 16)
                    w2_loaded[buf] = sw.n

                P.wait("tensor", sld, ldc)
                P.wait("tensor", sw, swc)
                ld_w1j(0, 0)
                for qc in range(4):
                    for jb in range(8):
                        buf = jb % 2
                        if qc * 8 + jb + 1 < 32:
                            ld_w1j((jb + 1) % 8, 1 - buf)
                        P.wait("tensor", sw, w1_loaded[buf])
                        for jj in range(4):
                            j = jb * 4 + jj
                            P.wait("tensor", sV, sV.n)
                            for kc in range(8):
                                P.op("tensor", lambda t, kc=kc, jj=jj, qc=qc, buf=buf: t.matmul(
                                    ph1[:],
                                    w1jb[buf][:, kc * 512 + jj * 128: kc * 512 + (jj + 1) * 128],
                                    xn1f_sb[:, kc * TOWN + qc * 512: kc * TOWN + (qc + 1) * 512],
                                    start=(kc == 0), stop=(kc == 7)), sT)
                            P.wait("vector", sT, sT.n)
                            P.op("vector", lambda v, j=j: v.tensor_scalar(
                                h1f_sb[:, j * 512:(j + 1) * 512], ph1[:],
                                b1c_sb[:, j:j + 1], 0.0, op0=ALU.add, op1=ALU.max), sV)
                        w1_free[buf] = sT.n
                    P.vsync(sV)
                    h1_ready = sV.n
                    P.wait("tensor", sV, h1_ready)
                    for rr in range(2):
                        for tt2 in range(2):
                            for half in range(2):
                                P.op("tensor", lambda t, tt2=tt2, half=half: t.matmul(
                                    yfs[tt2][:, half * 512:(half + 1) * 512],
                                    onesr_b[0:1, 0:128], b2r_sb[0:1, half * 512:(half + 1) * 512],
                                    start=True, stop=False), sT)
                        if qc == 0 and rr == 0:
                            ld_w2j(0, 0)
                        for j in range(32):
                            buf = j % 2
                            if (qc * 2 + rr) * 32 + j + 1 < 8 * 32:
                                ld_w2j((j + 1) % 32, 1 - buf)
                            P.wait("tensor", sw, w2_loaded[buf])
                            for tt2 in range(2):
                                ttl = rr * 2 + tt2
                                for half in range(2):
                                    P.op("tensor", lambda t, j=j, ttl=ttl, tt2=tt2, half=half,
                                         buf=buf: t.matmul(
                                        yfs[tt2][:, half * 512:(half + 1) * 512],
                                        h1f_sb[:, j * 512 + ttl * 128: j * 512 + ttl * 128 + 128],
                                        w2buf[buf][:, half * 512:(half + 1) * 512],
                                        start=False, stop=(j == 31)), sT)
                            w2_free[buf] = sT.n
                        sfc2_done = sT.n
                        for tt2 in range(2):
                            ttl = rr * 2 + tt2
                            tg = qc * 512 + ttl * 128
                            P.wait("sync", sV, sV.n)
                            P.op("sync", lambda s, tg=tg: s.dma_start(
                                out=ff_sb[:], in_=att1_sp[tg:tg + 128, :]), sld, 16)
                            P.wait("vector", sld, sld.n)
                            P.wait("vector", sT, sfc2_done)
                            P.op("vector", lambda v, tt2=tt2: v.tensor_add(
                                o_sb[:], ff_sb[:], yfs[tt2][:]), sV)
                            P.vdrain()
                            P.wait("sync", sV, sV.n)
                            P.op("sync", lambda s, tg=tg: s.dma_start(
                                out=out_own[tg:tg + 128, :], in_=o_sb[:]), sO, 16)
                            P.wait("vector", sO, sO.n)
                        P.wait("tensor", sO, sO.n)
            l1ph3.close()
            P.wait("sync", sO, sO.n)
            P.wait("sync", ssp, ssp.n)
            P.emit(block)
    return nc



def _l0_in_map(core, inputs):
    b, r = core // 4, core % 4
    x = inputs["L0_attn_x"][b]
    q0 = r * TOWN
    xrot = np.concatenate([x[q0:q0 + TOWN], x[q0 + TOWN:], x[:q0]], axis=0)
    return {
        "xr": np.ascontiguousarray(xrot),
        "wqkv": inputs["L0_attn_W_qkv"],
        "bqkv_cols": np.ascontiguousarray(inputs["L0_attn_b_qkv"][:2048].reshape(16, 128).T),
        "bqkv_vrow": inputs["L0_attn_b_qkv"][2048:].reshape(1, D),
        "wproj": inputs["L0_attn_W_proj"],
        "bproj_row": inputs["L0_attn_b_proj"].reshape(1, D),
        "gamma_cols": np.ascontiguousarray(inputs["L0_attn_ln_gamma"].reshape(8, 128).T),
        "beta_cols": np.ascontiguousarray(inputs["L0_attn_ln_beta"].reshape(8, 128).T),
        "gatewT": np.ascontiguousarray(inputs["L0_ffn_gate_w"].T),
        "fc1wT": np.ascontiguousarray(np.swapaxes(inputs["L0_ffn_fc1_w"], 1, 2)),
        "fc1b_xcols": np.ascontiguousarray(inputs["L0_ffn_fc1_b"][:, :F].reshape(E * 8, 128).T),
        "fc1b_gcols": np.ascontiguousarray(inputs["L0_ffn_fc1_b"][:, F:].reshape(E * 8, 128).T),
        "mult_cols": np.ascontiguousarray(inputs["L0_ffn_geglu_mult"].reshape(E * 8, 128).T),
        "fc2wT": np.ascontiguousarray(np.swapaxes(inputs["L0_ffn_fc2_w"], 1, 2)),
        "fc2b": inputs["L0_ffn_fc2_b"],
        "sfc1wT": np.ascontiguousarray(inputs["L0_ffn_s_fc1_w"].T),
        "sfc1b_xcols": np.ascontiguousarray(inputs["L0_ffn_s_fc1_b"][:FS].reshape(16, 128).T),
        "sfc1b_gcols": np.ascontiguousarray(inputs["L0_ffn_s_fc1_b"][FS:].reshape(16, 128).T),
        "smult_cols": np.ascontiguousarray(inputs["L0_ffn_s_geglu_mult"].reshape(16, 128).T),
        "sfc2wT": np.ascontiguousarray(inputs["L0_ffn_s_fc2_w"].T),
        "sfc2b_row": inputs["L0_ffn_s_fc2_b"].reshape(1, D),
        "iden": np.eye(128, dtype=np.float32),
        "sel": np.concatenate([np.tile(np.eye(E, dtype=np.float32)[:, e:e + 1], (1, 128))
                               for e in range(E)], axis=1),
    }


def _l1_in_map(core, inputs, x1):
    b, r = core // 4, core % 4
    x = x1[b]
    q0 = r * TOWN
    xrot = np.concatenate([x[q0:q0 + TOWN], x[q0 + TOWN:], x[:q0]], axis=0)
    mv = np.zeros((1, TALL), np.float32)
    mv[0, TOWN:TALL - q0] = -1e9
    bias = np.zeros((128, 8 * 64), np.float32)
    for qcq in range(8):
        for kt in range(16, 64):
            if kt * 128 < TALL - q0:
                bias[:, qcq * 64 + kt] = NEG
    pidx = np.arange(128)[:, None]
    fidx = np.arange(512)[None, :]
    diagA = np.concatenate(
        [np.where(fidx <= pidx + 128 * d, 0.0, -1e9).astype(np.float32) for d in range(4)], axis=1)
    f2 = np.arange(256)[None, :]
    diagB = np.concatenate(
        [np.where(pidx <= f2 - 128 * dd, 1.0, 0.0).astype(np.float32) for dd in range(2)], axis=1)
    return {
        "xr": np.ascontiguousarray(xrot),
        "wqk": np.ascontiguousarray(np.concatenate(
            [inputs["L1_attn_Wq"], inputs["L1_attn_Wk"]], axis=1)),
        "wv": inputs["L1_attn_Wv"],
        "w1": inputs["L1_ffn_W1"],
        "b1_cols": np.ascontiguousarray(inputs["L1_ffn_b1"].reshape(32, 128).T),
        "w2": inputs["L1_ffn_W2"],
        "b2_row": inputs["L1_ffn_b2"].reshape(1, D),
        "iden": np.eye(128, dtype=np.float32),
        "maskval": mv,
        "biascols": bias,
        "diagA": diagA,
        "diagB": diagB,
    }


def kernel(**inputs):
    inputs = {k: np.asarray(v) for k, v in inputs.items()}
    nc0 = build_l0()
    res0 = run_bass_kernel_spmd(nc0, [_l0_in_map(c, inputs) for c in range(8)],
                                core_ids=list(range(8)))
    x1 = np.zeros((2, TALL, D), np.float32)
    for c in range(8):
        b, r = c // 4, c % 4
        x1[b, r * TOWN:(r + 1) * TOWN] = res0.results[c]["x1own"]
    nc1 = build_l1()
    res1 = run_bass_kernel_spmd(nc1, [_l1_in_map(c, inputs, x1) for c in range(8)],
                                core_ids=list(range(8)))
    out = np.zeros((2, TALL, D), np.float32)
    for c in range(8):
        b, r = c // 4, c % 4
        out[b, r * TOWN:(r + 1) * TOWN] = res1.results[c]["out_own"]
    return out
